# version 01 (final)
# baseline (speedup 1.0000x reference)
import sys
sys.path.insert(0, "/opt/trn_rl_repo")
import contextlib
import numpy as np
import ml_dtypes
import concourse.bass as bass
import concourse.mybir as mybir
from concourse.bass_utils import run_bass_kernel_spmd

F32 = mybir.dt.float32
BF16 = mybir.dt.bfloat16
U8 = mybir.dt.uint8
AF = mybir.ActivationFunctionType
ALU = mybir.AluOpType
AX = mybir.AxisListType
BF = ml_dtypes.bfloat16
LN2 = float(np.log(2.0))

D = 2048
KT = 16
NCORE = 8
SCALING = float((2048 // 16) ** (-0.5))

C_NEG1 = 0
C_BIAS_LN2 = 1
C_SC_LN2H = 2
C_NEG128 = 3
C_EPS5 = 4
C_EPS6 = 5
NCONST = 6


def make_consts():
    c = np.zeros((128, NCONST), dtype=np.float32)
    c[:, C_NEG1] = -1.0
    c[:, C_BIAS_LN2] = -LN2
    c[:, C_SC_LN2H] = LN2 / 2.0
    c[:, C_NEG128] = -128.0
    c[:, C_EPS5] = 1e-5
    c[:, C_EPS6] = 1e-6
    return c


class Prog:

    def __init__(self, nc):
        self.nc = nc
        self.items = []

    def op(self, eng, fn):
        self.items.append((eng, fn, False))

    def dma(self, eng, fn):
        self.items.append((eng, fn, True))

    def finalize(self, block, sem):
        sched = []
        cnt = 0
        for eng, fn, dma in self.items:
            sched.append((eng, fn, dma, cnt))
            cnt += 16 if dma else 1
        total = cnt

        def mk(name):
            def body(eng):
                for en, fn, dma, wait in sched:
                    if en != name:
                        continue
                    eng.wait_ge(sem, wait)
                    fn().then_inc(sem, 16 if dma else 1)
                if name == "gpsimd":
                    eng.wait_ge(sem, total)
            return body

        block.sync(mk("sync"))
        block.scalar(mk("scalar"))
        block.vector(mk("vector"))
        block.tensor(mk("tensor"))
        block.gpsimd(mk("gpsimd"))


def _std_tensors(nc):
    consts = nc.dram_tensor("consts", [128, NCONST], F32, kind="ExternalInput").ap()
    ones_bf = nc.dram_tensor("ones_bf", [128, 1], BF16, kind="ExternalInput").ap()
    ones1 = nc.dram_tensor("ones1", [1, 128], F32, kind="ExternalInput").ap()
    ident_bf = nc.dram_tensor("ident_bf", [128, 128], BF16, kind="ExternalInput").ap()
    return consts, ones_bf, ones1, ident_bf


def make_std_arrays():
    return dict(
        consts=make_consts(),
        ones_bf=np.ones((128, 1), dtype=BF),
        ones1=np.ones((1, 128), dtype=np.float32),
        ident_bf=np.eye(128, dtype=BF),
    )



def build_W(FW):
    nc = bass.Bass("TRN2", target_bir_lowering=False)
    consts, ones_bf, ones1, ident_bf = _std_tensors(nc)
    flat_in = nc.dram_tensor("flat_in", [128, FW], F32, kind="ExternalInput").ap()
    flat_out = nc.dram_tensor("flat_out", [128, FW], BF16, kind="ExternalOutput").ap()
    g_pk = nc.dram_tensor("g_pk", [2048, 1024], U8, kind="ExternalInput").ap()
    g_sc = nc.dram_tensor("g_sc", [2048, 64], F32, kind="ExternalInput").ap()
    u_pk = nc.dram_tensor("u_pk", [2048, 1024], U8, kind="ExternalInput").ap()
    u_sc = nc.dram_tensor("u_sc", [2048, 64], F32, kind="ExternalInput").ap()
    d_pk = nc.dram_tensor("d_pk", [256, 8192], U8, kind="ExternalInput").ap()
    d_sc = nc.dram_tensor("d_sc", [256, 512], F32, kind="ExternalInput").ap()
    gT = nc.dram_tensor("gT", [2048, 2048], BF16, kind="ExternalOutput").ap()
    uT = nc.dram_tensor("uT", [2048, 2048], BF16, kind="ExternalOutput").ap()
    dT = nc.dram_tensor("dT", [16384, 256], BF16, kind="ExternalOutput").ap()

    CH = 8192

    with contextlib.ExitStack() as _es:
        fc32 = _es.enter_context(nc.sbuf_tensor("fc32", [128, CH], F32))
        fc16 = _es.enter_context(nc.sbuf_tensor("fc16", [128, CH], BF16))
        pkb = _es.enter_context(nc.sbuf_tensor("pkb", [128, 512], U8))
        nib = _es.enter_context(nc.sbuf_tensor("nib", [128, 512], U8))
        m0u = _es.enter_context(nc.sbuf_tensor("m0u", [128, 512], U8))
        teu = _es.enter_context(nc.sbuf_tensor("teu", [128, 512], U8))
        suu = _es.enter_context(nc.sbuf_tensor("suu", [128, 512], U8))
        msk = _es.enter_context(nc.sbuf_tensor("msk", [128, 512], U8))
        m0f = _es.enter_context(nc.sbuf_tensor("m0f", [128, 512], F32))
        sgn = _es.enter_context(nc.sbuf_tensor("sgn", [128, 512], F32))
        pf = _es.enter_context(nc.sbuf_tensor("pf", [128, 512], F32))
        hf = _es.enter_context(nc.sbuf_tensor("hf", [128, 512], F32))
        uf = _es.enter_context(nc.sbuf_tensor("uf", [128, 512], F32))
        af = _es.enter_context(nc.sbuf_tensor("af", [128, 512], F32))
        wnat = _es.enter_context(nc.sbuf_tensor("wnat", [128, 1024], F32))
        scal = _es.enter_context(nc.sbuf_tensor("scal", [128, 512], F32))
        sexp = _es.enter_context(nc.sbuf_tensor("sexp", [128, 16384], F32))
        wbf = _es.enter_context(nc.sbuf_tensor("wbf", [128, 16384], BF16))
        tev = _es.enter_context(nc.sbuf_tensor("tev", [128, 128], BF16))
        scon = _es.enter_context(nc.sbuf_tensor("scon", [128, NCONST], F32))
        sid = _es.enter_context(nc.sbuf_tensor("sid", [128, 128], BF16))
        ptr = _es.enter_context(nc.psum_tensor("ptr", [128, 128], BF16))
        po = _es.enter_context(nc.semaphore("po"))
        block = _es.enter_context(nc.Block())
        P = Prog(nc)
        P.dma("sync", lambda: nc.sync.dma_start(out=scon[:, :], in_=consts[:, :]))
        P.dma("sync", lambda: nc.sync.dma_start(out=sid[:, :], in_=ident_bf[:, :]))

        nch = FW // CH
        for i in range(nch):
            P.dma("sync", lambda i=i: nc.sync.dma_start(
                out=fc32[:, :], in_=flat_in[:, i * CH:(i + 1) * CH]))
            P.op("vector", lambda: nc.vector.tensor_copy(fc16[:, :], fc32[:, :]))
            P.dma("gpsimd", lambda i=i: nc.gpsimd.dma_start(
                out=flat_out[:, i * CH:(i + 1) * CH], in_=fc16[:, :]))

        def dequant_bytes(src_ap, KV):
            W = KV // 2
            for half, shift in ((0, None), (1, 4)):
                if shift is None:
                    P.op("vector", lambda W=W: nc.vector.tensor_single_scalar(
                        nib[:, :W], pkb[:, :W], 15, ALU.bitwise_and))
                else:
                    P.op("vector", lambda W=W: nc.vector.tensor_single_scalar(
                        nib[:, :W], pkb[:, :W], 4, ALU.logical_shift_right))
                P.op("vector", lambda W=W: nc.vector.tensor_single_scalar(
                    m0u[:, :W], nib[:, :W], 1, ALU.bitwise_and))
                P.op("vector", lambda W=W: nc.vector.tensor_single_scalar(
                    teu[:, :W], nib[:, :W], 6, ALU.bitwise_and))
                P.op("vector", lambda W=W: nc.vector.tensor_single_scalar(
                    suu[:, :W], nib[:, :W], 8, ALU.bitwise_and))
                P.op("vector", lambda W=W: nc.vector.tensor_single_scalar(
                    msk[:, :W], teu[:, :W], 0, ALU.is_equal))
                P.op("vector", lambda W=W: nc.vector.tensor_copy(m0f[:, :W], m0u[:, :W]))
                P.op("vector", lambda W=W: nc.vector.tensor_scalar(
                    sgn[:, :W], suu[:, :W], -0.25, 1.0, ALU.mult, ALU.add))
                P.op("scalar", lambda W=W: nc.scalar.activation(
                    pf[:, :W], teu[:, :W], AF.Exp,
                    bias=scon[:, C_BIAS_LN2:C_BIAS_LN2 + 1], scale=scon[:, C_SC_LN2H:C_SC_LN2H + 1]))
                P.op("vector", lambda W=W: nc.vector.tensor_scalar(
                    hf[:, :W], m0f[:, :W], 0.5, 1.0, ALU.mult, ALU.add))
                P.op("vector", lambda W=W: nc.vector.tensor_mul(uf[:, :W], hf[:, :W], pf[:, :W]))
                P.op("vector", lambda W=W: nc.vector.tensor_scalar_mul(af[:, :W], m0f[:, :W], 0.5))
                P.op("vector", lambda W=W: nc.vector.select(uf[:, :W], msk[:, :W], af[:, :W], uf[:, :W]))
                P.op("vector", lambda half=half, W=W: nc.vector.tensor_mul(
                    wnat[:, half:KV:2], uf[:, :W], sgn[:, :W]))

        def do_weight(pk_ap, sc_ap, out_ap, NP, KV):
            nbt = KV // 1024
            for npt in range(NP):
                nsc = KV // 32
                P.dma("sync", lambda npt=npt, nsc=nsc: nc.sync.dma_start(
                    out=scal[:, :nsc], in_=sc_ap[npt * 128:(npt + 1) * 128, :]))
                for j in range(32):
                    P.op("vector", lambda j=j, nsc=nsc: nc.vector.tensor_copy(
                        sexp[:, j:KV:32], scal[:, :nsc]))
                for bt in range(nbt):
                    P.dma("sync", lambda npt=npt, bt=bt: nc.sync.dma_start(
                        out=pkb[:, :512], in_=pk_ap[npt * 128:(npt + 1) * 128, bt * 512:(bt + 1) * 512]))
                    dequant_bytes(None, 1024)
                    P.op("vector", lambda bt=bt: nc.vector.tensor_tensor(
                        wbf[:, bt * 1024:(bt + 1) * 1024], wnat[:, 0:1024],
                        sexp[:, bt * 1024:(bt + 1) * 1024], ALU.mult))
                for j in range(KV // 128):
                    P.op("tensor", lambda j=j: nc.tensor.transpose(
                        ptr[:, :], wbf[:, j * 128:(j + 1) * 128], sid[:, :]))
                    P.op("vector", lambda: nc.vector.tensor_copy(tev[:, :], ptr[:, :]))
                    P.dma("gpsimd", lambda j=j, npt=npt: nc.gpsimd.dma_start(
                        out=out_ap[j * 128:(j + 1) * 128, npt * 128:(npt + 1) * 128], in_=tev[:, :]))

        do_weight(g_pk, g_sc, gT, 16, 2048)
        do_weight(u_pk, u_sc, uT, 16, 2048)
        do_weight(d_pk, d_sc, dT, 2, 16384)

        P.finalize(block, po)
    return nc



def emit_stats(P, nc, ctx, src_bf, psS, psQ, psB1, psB2, ones_bf, scon, eps_col,
               mean=True, src_sq_from=None):
    TS = ctx.TS
    sq = ctx.tb
    for kt in range(KT):
        P.op("scalar", lambda kt=kt: nc.scalar.activation(
            sq[:, kt * TS:(kt + 1) * TS], src_bf[kt], AF.Square))
    for kt in range(KT):
        if mean:
            P.op("tensor", lambda kt=kt: nc.tensor.matmul(
                psS[0:1, :TS], ones_bf[:, :], src_bf[kt], start=(kt == 0), stop=(kt == KT - 1)))
        P.op("tensor", lambda kt=kt: nc.tensor.matmul(
            psQ[0:1, :TS], ones_bf[:, :], sq[:, kt * TS:(kt + 1) * TS],
            start=(kt == 0), stop=(kt == KT - 1)))
    srow = ctx.srow
    if mean:
        P.op("vector", lambda: nc.vector.tensor_scalar_mul(srow[:, 0:TS], psS[0:1, :TS], 1.0 / D))
    P.op("vector", lambda: nc.vector.tensor_scalar_mul(srow[:, TS:2 * TS], psQ[0:1, :TS], 1.0 / D))
    if mean:
        P.op("vector", lambda: nc.vector.tensor_mul(srow[:, 2 * TS:3 * TS], srow[:, 0:TS], srow[:, 0:TS]))
        P.op("vector", lambda: nc.vector.tensor_sub(srow[:, 2 * TS:3 * TS], srow[:, TS:2 * TS], srow[:, 2 * TS:3 * TS]))
    else:
        P.op("vector", lambda: nc.vector.tensor_copy(srow[:, 2 * TS:3 * TS], srow[:, TS:2 * TS]))
    P.op("scalar", lambda: nc.scalar.activation(
        srow[:, 3 * TS:4 * TS], srow[:, 2 * TS:3 * TS], AF.Sqrt, bias=scon[0:1, eps_col:eps_col + 1]))
    P.op("vector", lambda: nc.vector.reciprocal(srow[:, 3 * TS:4 * TS], srow[:, 3 * TS:4 * TS]))
    if mean:
        P.op("vector", lambda: nc.vector.tensor_mul(srow[:, 4 * TS:5 * TS], srow[:, 0:TS], srow[:, 3 * TS:4 * TS]))
    P.op("tensor", lambda: nc.tensor.matmul(
        psB1[:, :TS], P.ones1[:, :], srow[:, 3 * TS:4 * TS], start=True, stop=True))
    if mean:
        P.op("tensor", lambda: nc.tensor.matmul(
            psB2[:, :TS], P.ones1[:, :], srow[:, 4 * TS:5 * TS], start=True, stop=True))



def build_L1(TLOC):
    nc = bass.Bass("TRN2", target_bir_lowering=False)
    consts, ones_bf, ones1, ident_bf = _std_tensors(nc)
    xT = nc.dram_tensor("xT", [D, TLOC], F32, kind="ExternalInput").ap()
    Wq = nc.dram_tensor("Wq", [D, 1024], BF16, kind="ExternalInput").ap()
    Wk = nc.dram_tensor("Wk", [D, 1024], BF16, kind="ExternalInput").ap()
    Wkg1 = nc.dram_tensor("Wkg1", [D, 16], BF16, kind="ExternalInput").ap()
    Wkg2 = nc.dram_tensor("Wkg2", [16, 1024], BF16, kind="ExternalInput").ap()
    Wv = nc.dram_tensor("Wv", [D, D], BF16, kind="ExternalInput").ap()
    Wg = nc.dram_tensor("Wg", [D, D], BF16, kind="ExternalInput").ap()
    ln1c = nc.dram_tensor("ln1c", [128, 16], F32, kind="ExternalInput").ap()
    bgc = nc.dram_tensor("bgc", [128, 16], F32, kind="ExternalInput").ap()
    bkgc = nc.dram_tensor("bkgc", [128, 8], F32, kind="ExternalInput").ap()
    q_o = nc.dram_tensor("q_o", [1024, TLOC], BF16, kind="ExternalOutput").ap()
    k_o = nc.dram_tensor("k_o", [1024, TLOC], BF16, kind="ExternalOutput").ap()
    gk_o = nc.dram_tensor("gk_o", [1024, TLOC], F32, kind="ExternalOutput").ap()
    v_o = nc.dram_tensor("v_o", [D, TLOC], BF16, kind="ExternalOutput").ap()
    sg_o = nc.dram_tensor("sg_o", [D, TLOC], BF16, kind="ExternalOutput").ap()

    TS = min(512, TLOC)
    NTC = TLOC // TS

    with contextlib.ExitStack() as _es:
        xb = _es.enter_context(nc.sbuf_tensor("xb", [128, KT * TS], F32))
        xnb = _es.enter_context(nc.sbuf_tensor("xnb", [128, KT * TS], BF16))
        wsl = _es.enter_context(nc.sbuf_tensor("wsl", [128, 512], BF16))
        zb = _es.enter_context(nc.sbuf_tensor("zb", [16, TS], BF16))
        ev = _es.enter_context(nc.sbuf_tensor("ev", [128, TS], BF16))
        ef = _es.enter_context(nc.sbuf_tensor("ef", [128, TS], F32))
        sln1 = _es.enter_context(nc.sbuf_tensor("sln1", [128, 16], F32))
        sbg = _es.enter_context(nc.sbuf_tensor("sbg", [128, 16], F32))
        sbkg = _es.enter_context(nc.sbuf_tensor("sbkg", [128, 8], F32))
        sones_bf = _es.enter_context(nc.sbuf_tensor("sones_bf", [128, 1], BF16))
        sones1 = _es.enter_context(nc.sbuf_tensor("sones1", [1, 128], F32))
        scon = _es.enter_context(nc.sbuf_tensor("scon", [128, NCONST], F32))
        ln = type("LN", (), {})()
        ln.TS = TS
        ln.tb = _es.enter_context(nc.sbuf_tensor("ln_tb_" + "a", [128, KT * TS], BF16))
        ln.srow = _es.enter_context(nc.sbuf_tensor("ln_sr_" + "a", [1, 6 * TS], F32))
        pA0 = _es.enter_context(nc.psum_tensor("pA0", [128, TS], F32))
        pA1 = _es.enter_context(nc.psum_tensor("pA1", [128, TS], F32))
        pA2 = _es.enter_context(nc.psum_tensor("pA2", [128, TS], F32))
        pA3 = _es.enter_context(nc.psum_tensor("pA3", [128, TS], F32))
        pB1 = _es.enter_context(nc.psum_tensor("pB1", [128, TS], F32))
        pB2 = _es.enter_context(nc.psum_tensor("pB2", [128, TS], F32))
        pD = _es.enter_context(nc.psum_tensor("pD", [128, TS], F32))
        po = _es.enter_context(nc.semaphore("po"))
        block = _es.enter_context(nc.Block())
        P = Prog(nc)
        P.ones1 = sones1
        pAs = [pA0, pA1, pA2, pA3]
        P.dma("sync", lambda: nc.sync.dma_start(out=scon[:, :], in_=consts[:, :]))
        P.dma("sync", lambda: nc.sync.dma_start(out=sones_bf[:, :], in_=ones_bf[:, :]))
        P.dma("sync", lambda: nc.sync.dma_start(out=sones1[:, :], in_=ones1[:, :]))
        P.dma("sync", lambda: nc.sync.dma_start(out=sln1[:, :], in_=ln1c[:, :]))
        P.dma("sync", lambda: nc.sync.dma_start(out=sbg[:, :], in_=bgc[:, :]))
        P.dma("sync", lambda: nc.sync.dma_start(out=sbkg[:, :], in_=bkgc[:, :]))

        for tc in range(NTC):
            t0 = tc * TS
            for kt in range(KT):
                P.dma("sync", lambda kt=kt, t0=t0: nc.sync.dma_start(
                    out=xb[:, kt * TS:(kt + 1) * TS], in_=xT[kt * 128:(kt + 1) * 128, t0:t0 + TS]))
            for kt in range(KT):
                P.op("vector", lambda kt=kt: nc.vector.tensor_copy(
                    xnb[:, kt * TS:(kt + 1) * TS], xb[:, kt * TS:(kt + 1) * TS]))
            src_bf = [xnb[:, kt * TS:(kt + 1) * TS] for kt in range(KT)]
            emit_stats(P, nc, ln, src_bf, pD, pA0, pB1, pB2, sones_bf, scon, C_EPS5, mean=True)
            for kt in range(KT):
                P.op("vector", lambda kt=kt: nc.vector.tensor_tensor(
                    xb[:, kt * TS:(kt + 1) * TS], xb[:, kt * TS:(kt + 1) * TS], pB1[:, :TS], ALU.mult))
            for kt in range(KT):
                P.op("vector", lambda kt=kt: nc.vector.tensor_tensor(
                    xb[:, kt * TS:(kt + 1) * TS], xb[:, kt * TS:(kt + 1) * TS], pB2[:, :TS], ALU.subtract))
            for kt in range(KT):
                P.op("vector", lambda kt=kt: nc.vector.tensor_copy(
                    xnb[:, kt * TS:(kt + 1) * TS], xb[:, kt * TS:(kt + 1) * TS]))
            emit_stats(P, nc, ln, src_bf, pD, pA0, pB1, pB2, sones_bf, scon, C_EPS5, mean=False)
            for kt in range(KT):
                P.op("vector", lambda kt=kt: nc.vector.scalar_tensor_tensor(
                    xnb[:, kt * TS:(kt + 1) * TS], xb[:, kt * TS:(kt + 1) * TS],
                    sln1[:, kt:kt + 1], pB1[:, :TS], ALU.mult, ALU.mult))

            def proj(W_ap, N, out_ap, post, t0=t0):
                for ns in range(N // 512):
                    for kt in range(KT):
                        P.dma("sync", lambda kt=kt, ns=ns, W_ap=W_ap: nc.sync.dma_start(
                            out=wsl[:, :], in_=W_ap[kt * 128:(kt + 1) * 128, ns * 512:(ns + 1) * 512]))
                        for sub in range(4):
                            P.op("tensor", lambda kt=kt, sub=sub: nc.tensor.matmul(
                                pAs[sub][:, :TS], wsl[:, sub * 128:(sub + 1) * 128],
                                xnb[:, kt * TS:(kt + 1) * TS],
                                start=(kt == 0), stop=(kt == KT - 1)))
                    for sub in range(4):
                        nt = ns * 4 + sub
                        post(nt, pAs[sub], out_ap)

            def post_q(nt, ps, out_ap, t0=t0):
                P.op("vector", lambda ps=ps: nc.vector.tensor_copy(ev[:, :TS], ps[:, :TS]))
                P.dma("gpsimd", lambda nt=nt, t0=t0, out_ap=out_ap: nc.gpsimd.dma_start(
                    out=out_ap[nt * 128:(nt + 1) * 128, t0:t0 + TS], in_=ev[:, :TS]))

            def post_k(nt, ps, out_ap, t0=t0):
                P.op("vector", lambda ps=ps: nc.vector.tensor_scalar_mul(ev[:, :TS], ps[:, :TS], SCALING))
                P.dma("gpsimd", lambda nt=nt, t0=t0, out_ap=out_ap: nc.gpsimd.dma_start(
                    out=out_ap[nt * 128:(nt + 1) * 128, t0:t0 + TS], in_=ev[:, :TS]))

            def post_g(nt, ps, out_ap, t0=t0):
                P.op("scalar", lambda ps=ps, nt=nt: nc.scalar.activation(
                    ev[:, :TS], ps[:, :TS], AF.Silu, bias=sbg[:, nt:nt + 1]))
                P.dma("gpsimd", lambda nt=nt, t0=t0, out_ap=out_ap: nc.gpsimd.dma_start(
                    out=out_ap[nt * 128:(nt + 1) * 128, t0:t0 + TS], in_=ev[:, :TS]))

            proj(Wq, 1024, q_o, post_q)
            proj(Wk, 1024, k_o, post_k)
            proj(Wv, D, v_o, post_q)
            proj(Wg, D, sg_o, post_g)

            for kt in range(KT):
                P.dma("sync", lambda kt=kt: nc.sync.dma_start(
                    out=wsl[:, 0:16], in_=Wkg1[kt * 128:(kt + 1) * 128, :]))
                P.op("tensor", lambda kt=kt: nc.tensor.matmul(
                    pD[0:16, :TS], wsl[:, 0:16], xnb[:, kt * TS:(kt + 1) * TS],
                    start=(kt == 0), stop=(kt == KT - 1)))
            P.op("vector", lambda: nc.vector.tensor_copy(zb[:, :TS], pD[0:16, :TS]))
            for nt in range(8):
                P.dma("sync", lambda nt=nt: nc.sync.dma_start(
                    out=wsl[0:16, 0:128], in_=Wkg2[0:16, nt * 128:(nt + 1) * 128]))
                P.op("tensor", lambda nt=nt: nc.tensor.matmul(
                    pA0[:, :TS], wsl[0:16, 0:128], zb[:, :TS], start=True, stop=True))
                P.op("scalar", lambda nt=nt: nc.scalar.activation(
                    ef[:, :TS], pA0[:, :TS], AF.Sigmoid, bias=sbkg[:, nt:nt + 1]))
                P.op("scalar", lambda: nc.scalar.activation(ef[:, :TS], ef[:, :TS], AF.Ln))
                P.op("vector", lambda: nc.vector.tensor_scalar_mul(ef[:, :TS], ef[:, :TS], 1.0 / 16.0))
                P.dma("gpsimd", lambda nt=nt, t0=t0: nc.gpsimd.dma_start(
                    out=gk_o[nt * 128:(nt + 1) * 128, t0:t0 + TS], in_=ef[:, :TS]))
        P.finalize(block, po)
    return nc



def build_L2(NSEQ):
    nc = bass.Bass("TRN2", target_bir_lowering=False)
    consts, ones_bf, ones1, ident_bf = _std_tensors(nc)
    maskUT = nc.dram_tensor("maskUT", [128, 128], F32, kind="ExternalInput").ap()
    ones_sc = nc.dram_tensor("ones_sc", [64, 128], F32, kind="ExternalInput").ap()
    q_i = nc.dram_tensor("q_i", [4, 64, NSEQ], BF16, kind="ExternalInput").ap()
    k_i = nc.dram_tensor("k_i", [4, 64, NSEQ], BF16, kind="ExternalInput").ap()
    gk_i = nc.dram_tensor("gk_i", [4, 64, NSEQ], F32, kind="ExternalInput").ap()
    v_i = nc.dram_tensor("v_i", [4, 128, NSEQ], BF16, kind="ExternalInput").ap()
    o_o = nc.dram_tensor("o_o", [4, NSEQ, 128], BF16, kind="ExternalOutput").ap()

    NCH = NSEQ // 128

    with contextlib.ExitStack() as _es:
        sq = _es.enter_context(nc.sbuf_tensor("sq", [64, 128], BF16))
        sk = _es.enter_context(nc.sbuf_tensor("sk", [64, 128], BF16))
        sgk = _es.enter_context(nc.sbuf_tensor("sgk", [64, 128], F32))
        sv = _es.enter_context(nc.sbuf_tensor("sv", [128, 128], BF16))
        sA = _es.enter_context(nc.sbuf_tensor("sA", [64, 128], F32))
        sE = _es.enter_context(nc.sbuf_tensor("sE", [64, 128], F32))
        sqt = _es.enter_context(nc.sbuf_tensor("sqt", [64, 128], BF16))
        skt = _es.enter_context(nc.sbuf_tensor("skt", [64, 128], BF16))
        sk2 = _es.enter_context(nc.sbuf_tensor("sk2", [64, 128], BF16))
        scm = _es.enter_context(nc.sbuf_tensor("scm", [128, 128], BF16))
        svt = _es.enter_context(nc.sbuf_tensor("svt", [128, 128], BF16))
        sk2t = _es.enter_context(nc.sbuf_tensor("sk2t", [128, 64], BF16))
        sS = _es.enter_context(nc.sbuf_tensor("sS", [64, 128], F32))
        sSA = _es.enter_context(nc.sbuf_tensor("sSA", [64, 128], BF16))
        scol = _es.enter_context(nc.sbuf_tensor("scol", [64, 6], F32))
        sEb = _es.enter_context(nc.sbuf_tensor("sEb", [64, 128], BF16))
        so2 = _es.enter_context(nc.sbuf_tensor("so2", [128, 128], F32))
        son = _es.enter_context(nc.sbuf_tensor("son", [128, 128], BF16))
        srw = _es.enter_context(nc.sbuf_tensor("srow", [128, 8], F32))
        smask = _es.enter_context(nc.sbuf_tensor("smask", [128, 128], F32))
        sonesc = _es.enter_context(nc.sbuf_tensor("sonesc", [64, 128], F32))
        sid = _es.enter_context(nc.sbuf_tensor("sid", [128, 128], BF16))
        scon = _es.enter_context(nc.sbuf_tensor("scon", [128, NCONST], F32))
        psc = _es.enter_context(nc.psum_tensor("psc", [128, 128], F32))
        pvt = _es.enter_context(nc.psum_tensor("pvt", [128, 128], BF16))
        po_ = _es.enter_context(nc.psum_tensor("po_", [128, 128], F32))
        pdS = _es.enter_context(nc.psum_tensor("pdS", [64, 128], F32))
        pk2t = _es.enter_context(nc.psum_tensor("pk2t", [128, 64], BF16))
        po = _es.enter_context(nc.semaphore("po"))
        block = _es.enter_context(nc.Block())
        P = Prog(nc)
        P.dma("sync", lambda: nc.sync.dma_start(out=scon[:, :], in_=consts[:, :]))
        P.dma("sync", lambda: nc.sync.dma_start(out=smask[:, :], in_=maskUT[:, :]))
        P.dma("sync", lambda: nc.sync.dma_start(out=sonesc[:, :], in_=ones_sc[:, :]))
        P.dma("sync", lambda: nc.sync.dma_start(out=sid[:, :], in_=ident_bf[:, :]))

        for st in range(4):
            P.op("vector", lambda: nc.vector.memset(sS[:, :], 0.0))
            for ch in range(NCH):
                c0 = ch * 128
                P.dma("sync", lambda st=st, c0=c0: nc.sync.dma_start(
                    out=sq[:, :], in_=q_i[st, :, c0:c0 + 128]))
                P.dma("sync", lambda st=st, c0=c0: nc.sync.dma_start(
                    out=sk[:, :], in_=k_i[st, :, c0:c0 + 128]))
                P.dma("sync", lambda st=st, c0=c0: nc.sync.dma_start(
                    out=sgk[:, :], in_=gk_i[st, :, c0:c0 + 128]))
                P.dma("sync", lambda st=st, c0=c0: nc.sync.dma_start(
                    out=sv[:, :], in_=v_i[st, :, c0:c0 + 128]))
                P.op("vector", lambda: nc.vector.tensor_tensor_scan(
                    sA[:, :], sonesc[:, :], sgk[:, :], 0.0, ALU.mult, ALU.add))
                P.op("vector", lambda: nc.vector.tensor_copy(scol[:, 3:4], sA[:, 64:65]))
                P.op("vector", lambda: nc.vector.tensor_copy(scol[:, 1:2], sA[:, 127:128]))
                P.op("vector", lambda: nc.vector.tensor_sub(scol[:, 2:3], scol[:, 1:2], scol[:, 3:4]))
                P.op("scalar", lambda: nc.scalar.activation(scol[:, 0:1], scol[:, 3:4], AF.Exp))
                P.op("scalar", lambda: nc.scalar.activation(scol[:, 1:2], scol[:, 1:2], AF.Exp))
                P.op("scalar", lambda: nc.scalar.activation(scol[:, 2:3], scol[:, 2:3], AF.Exp))
                P.op("vector", lambda: nc.vector.tensor_scalar(
                    sA[:, :], sA[:, :], scol[:, 3:4], None, ALU.subtract))
                P.op("scalar", lambda: nc.scalar.activation(sE[:, :], sA[:, :], AF.Exp))
                P.op("vector", lambda: nc.vector.tensor_copy(sEb[:, :], sE[:, :]))
                P.op("vector", lambda: nc.vector.tensor_mul(sqt[:, :], sq[:, :], sEb[:, :]))
                P.op("scalar", lambda: nc.scalar.activation(
                    sE[:, :], sA[:, :], AF.Exp, scale=scon[0:64, C_NEG1:C_NEG1 + 1]))
                P.op("vector", lambda: nc.vector.tensor_copy(sEb[:, :], sE[:, :]))
                P.op("vector", lambda: nc.vector.tensor_mul(skt[:, :], sk[:, :], sEb[:, :]))
                P.op("vector", lambda: nc.vector.tensor_scalar(
                    sk2[:, :], skt[:, :], scol[:, 2:3], None, ALU.mult))
                P.op("vector", lambda: nc.vector.tensor_scalar(
                    sSA[:, :], sS[:, :], scol[:, 0:1], None, ALU.mult))
                P.op("tensor", lambda: nc.tensor.matmul(
                    psc[:, :], skt[:, :], sqt[:, :], start=True, stop=True))
                P.op("vector", lambda: nc.vector.tensor_tensor(
                    scm[:, :], psc[:, :], smask[:, :], ALU.mult))
                P.op("tensor", lambda: nc.tensor.transpose(pvt[:, :], sv[:, :], sid[:, :]))
                P.op("vector", lambda: nc.vector.tensor_copy(svt[:, :], pvt[:, :]))
                P.op("tensor", lambda: nc.tensor.matmul(
                    po_[:, :], scm[:, :], svt[:, :], start=True, stop=False))
                P.op("tensor", lambda: nc.tensor.matmul(
                    po_[:, :], sqt[:, :], sSA[:, :], start=False, stop=True))
                P.op("vector", lambda: nc.vector.reduce_sum(srw[:, 0:1], po_[:, :], axis=AX.X))
                P.op("scalar", lambda: nc.scalar.activation(so2[:, :], po_[:, :], AF.Square))
                P.op("vector", lambda: nc.vector.reduce_sum(srw[:, 1:2], so2[:, :], axis=AX.X))
                P.op("vector", lambda: nc.vector.tensor_scalar_mul(srw[:, 0:1], srw[:, 0:1], 1.0 / 128.0))
                P.op("vector", lambda: nc.vector.tensor_scalar_mul(srw[:, 1:2], srw[:, 1:2], 1.0 / 128.0))
                P.op("vector", lambda: nc.vector.tensor_mul(srw[:, 2:3], srw[:, 0:1], srw[:, 0:1]))
                P.op("vector", lambda: nc.vector.tensor_sub(srw[:, 1:2], srw[:, 1:2], srw[:, 2:3]))
                P.op("scalar", lambda: nc.scalar.activation(
                    srw[:, 1:2], srw[:, 1:2], AF.Sqrt, bias=scon[:, C_EPS5:C_EPS5 + 1]))
                P.op("vector", lambda: nc.vector.reciprocal(srw[:, 1:2], srw[:, 1:2]))
                P.op("vector", lambda: nc.vector.tensor_scalar(
                    son[:, :], po_[:, :], srw[:, 0:1], srw[:, 1:2], ALU.subtract, ALU.mult))
                P.dma("gpsimd", lambda st=st, c0=c0: nc.gpsimd.dma_start(
                    out=o_o[st, c0:c0 + 128, :], in_=son[:, :]))
                P.op("tensor", lambda: nc.tensor.transpose(pk2t[:, :], sk2[:, :], sid[0:64, 0:64]))
                P.op("vector", lambda: nc.vector.tensor_copy(sk2t[:, :], pk2t[:, :]))
                P.op("tensor", lambda: nc.tensor.matmul(
                    pdS[:, :], sk2t[:, :], svt[:, :], start=True, stop=True))
                P.op("vector", lambda: nc.vector.tensor_scalar(
                    sS[:, :], sS[:, :], scol[:, 1:2], None, ALU.mult))
                P.op("vector", lambda: nc.vector.tensor_tensor(
                    sS[:, :], sS[:, :], pdS[:, :], ALU.add))
        P.finalize(block, po)
    return nc



def build_L3(TLOC):
    nc = bass.Bass("TRN2", target_bir_lowering=False)
    consts, ones_bf, ones1, ident_bf = _std_tensors(nc)
    xT = nc.dram_tensor("xT", [D, TLOC], F32, kind="ExternalInput").ap()
    o_i = nc.dram_tensor("o_i", [D, TLOC], BF16, kind="ExternalInput").ap()
    sg_i = nc.dram_tensor("sg_i", [D, TLOC], BF16, kind="ExternalInput").ap()
    Wo = nc.dram_tensor("Wo", [D, D], BF16, kind="ExternalInput").ap()
    W1 = nc.dram_tensor("W1", [D, 5632], BF16, kind="ExternalInput").ap()
    W3 = nc.dram_tensor("W3", [D, 5632], BF16, kind="ExternalInput").ap()
    W2 = nc.dram_tensor("W2", [5632, D], BF16, kind="ExternalInput").ap()
    ln2c = nc.dram_tensor("ln2c", [128, 16], F32, kind="ExternalInput").ap()
    x1_o = nc.dram_tensor("x1_o", [D, TLOC], F32, kind="ExternalOutput").ap()

    TS = min(256, TLOC)
    NTC = TLOC // TS
    HK = 44

    with contextlib.ExitStack() as _es:
        xb = _es.enter_context(nc.sbuf_tensor("xb", [128, KT * TS], F32))
        xr = _es.enter_context(nc.sbuf_tensor("xr", [128, KT * TS], F32))
        onb = _es.enter_context(nc.sbuf_tensor("onb", [128, KT * TS], BF16))
        sgb = _es.enter_context(nc.sbuf_tensor("sgb", [128, KT * TS], BF16))
        xnb = _es.enter_context(nc.sbuf_tensor("xnb", [128, KT * TS], BF16))
        hm = _es.enter_context(nc.sbuf_tensor("hm", [128, HK * TS], BF16))
        wsl = _es.enter_context(nc.sbuf_tensor("wsl", [128, 512], BF16))
        ev = _es.enter_context(nc.sbuf_tensor("ev", [128, TS], BF16))
        sln2 = _es.enter_context(nc.sbuf_tensor("sln2", [128, 16], F32))
        sones_bf = _es.enter_context(nc.sbuf_tensor("sones_bf", [128, 1], BF16))
        sones1 = _es.enter_context(nc.sbuf_tensor("sones1", [1, 128], F32))
        scon = _es.enter_context(nc.sbuf_tensor("scon", [128, NCONST], F32))
        ln = type("LN", (), {})()
        ln.TS = TS
        ln.tb = _es.enter_context(nc.sbuf_tensor("ln_tb_" + "c", [128, KT * TS], BF16))
        ln.srow = _es.enter_context(nc.sbuf_tensor("ln_sr_" + "c", [1, 6 * TS], F32))
        pA0 = _es.enter_context(nc.psum_tensor("pA0", [128, TS], F32))
        pA1 = _es.enter_context(nc.psum_tensor("pA1", [128, TS], F32))
        pA2 = _es.enter_context(nc.psum_tensor("pA2", [128, TS], F32))
        pA3 = _es.enter_context(nc.psum_tensor("pA3", [128, TS], F32))
        pB1 = _es.enter_context(nc.psum_tensor("pB1", [128, TS], F32))
        pB2 = _es.enter_context(nc.psum_tensor("pB2", [128, TS], F32))
        pD = _es.enter_context(nc.psum_tensor("pD", [128, TS], F32))
        po = _es.enter_context(nc.semaphore("po"))
        block = _es.enter_context(nc.Block())
        P = Prog(nc)
        P.ones1 = sones1
        pAs = [pA0, pA1, pA2, pA3]
        P.dma("sync", lambda: nc.sync.dma_start(out=scon[:, :], in_=consts[:, :]))
        P.dma("sync", lambda: nc.sync.dma_start(out=sones_bf[:, :], in_=ones_bf[:, :]))
        P.dma("sync", lambda: nc.sync.dma_start(out=sones1[:, :], in_=ones1[:, :]))
        P.dma("sync", lambda: nc.sync.dma_start(out=sln2[:, :], in_=ln2c[:, :]))

        def loadbuf(buf, src, t0):
            for kt in range(KT):
                P.dma("sync", lambda kt=kt, buf=buf, src=src, t0=t0: nc.sync.dma_start(
                    out=buf[:, kt * TS:(kt + 1) * TS], in_=src[kt * 128:(kt + 1) * 128, t0:t0 + TS]))

        for tc in range(NTC):
            t0 = tc * TS
            loadbuf(xb, xT, t0)
            loadbuf(onb, o_i, t0)
            loadbuf(sgb, sg_i, t0)
            for kt in range(KT):
                P.op("vector", lambda kt=kt: nc.vector.tensor_copy(
                    xnb[:, kt * TS:(kt + 1) * TS], xb[:, kt * TS:(kt + 1) * TS]))
            src_bf = [xnb[:, kt * TS:(kt + 1) * TS] for kt in range(KT)]
            emit_stats(P, nc, ln, src_bf, pD, pA0, pB1, pB2, sones_bf, scon, C_EPS5, mean=True)
            for kt in range(KT):
                P.op("vector", lambda kt=kt: nc.vector.tensor_tensor(
                    xb[:, kt * TS:(kt + 1) * TS], xb[:, kt * TS:(kt + 1) * TS], pB1[:, :TS], ALU.mult))
            for kt in range(KT):
                P.op("vector", lambda kt=kt: nc.vector.tensor_tensor(
                    xb[:, kt * TS:(kt + 1) * TS], xb[:, kt * TS:(kt + 1) * TS], pB2[:, :TS], ALU.subtract))
            for kt in range(KT):
                P.op("vector", lambda kt=kt: nc.vector.tensor_mul(
                    onb[:, kt * TS:(kt + 1) * TS], onb[:, kt * TS:(kt + 1) * TS],
                    sgb[:, kt * TS:(kt + 1) * TS]))
            for ns in range(4):
                for kt in range(KT):
                    P.dma("sync", lambda kt=kt, ns=ns: nc.sync.dma_start(
                        out=wsl[:, :], in_=Wo[kt * 128:(kt + 1) * 128, ns * 512:(ns + 1) * 512]))
                    for sub in range(4):
                        P.op("tensor", lambda kt=kt, sub=sub: nc.tensor.matmul(
                            pAs[sub][:, :TS], wsl[:, sub * 128:(sub + 1) * 128],
                            onb[:, kt * TS:(kt + 1) * TS], start=(kt == 0), stop=(kt == KT - 1)))
                for sub in range(4):
                    nt = ns * 4 + sub
                    P.op("vector", lambda nt=nt, sub=sub: nc.vector.tensor_tensor(
                        xb[:, nt * TS:(nt + 1) * TS], xb[:, nt * TS:(nt + 1) * TS],
                        pAs[sub][:, :TS], ALU.add))
            for kt in range(KT):
                P.op("vector", lambda kt=kt: nc.vector.tensor_copy(
                    xnb[:, kt * TS:(kt + 1) * TS], xb[:, kt * TS:(kt + 1) * TS]))
            emit_stats(P, nc, ln, src_bf, pD, pA0, pB1, pB2, sones_bf, scon, C_EPS5, mean=False)
            for kt in range(KT):
                P.op("vector", lambda kt=kt: nc.vector.scalar_tensor_tensor(
                    xnb[:, kt * TS:(kt + 1) * TS], xb[:, kt * TS:(kt + 1) * TS],
                    sln2[:, kt:kt + 1], pB1[:, :TS], ALU.mult, ALU.mult))
            for ns in range(22):
                for kt in range(KT):
                    P.dma("sync", lambda kt=kt, ns=ns: nc.sync.dma_start(
                        out=wsl[:, 0:256], in_=W1[kt * 128:(kt + 1) * 128, ns * 256:(ns + 1) * 256]))
                    for sub in range(2):
                        P.op("tensor", lambda kt=kt, sub=sub: nc.tensor.matmul(
                            pAs[sub][:, :TS], wsl[:, sub * 128:(sub + 1) * 128],
                            xnb[:, kt * TS:(kt + 1) * TS], start=(kt == 0), stop=(kt == KT - 1)))
                for kt in range(KT):
                    P.dma("sync", lambda kt=kt, ns=ns: nc.sync.dma_start(
                        out=wsl[:, 0:256], in_=W3[kt * 128:(kt + 1) * 128, ns * 256:(ns + 1) * 256]))
                    for sub in range(2):
                        P.op("tensor", lambda kt=kt, sub=sub: nc.tensor.matmul(
                            pAs[2 + sub][:, :TS], wsl[:, sub * 128:(sub + 1) * 128],
                            xnb[:, kt * TS:(kt + 1) * TS], start=(kt == 0), stop=(kt == KT - 1)))
                for sub in range(2):
                    ht = ns * 2 + sub
                    P.op("scalar", lambda sub=sub: nc.scalar.activation(
                        ev[:, :TS], pAs[sub][:, :TS], AF.Silu))
                    P.op("vector", lambda ht=ht, sub=sub: nc.vector.tensor_tensor(
                        hm[:, ht * TS:(ht + 1) * TS], ev[:, :TS], pAs[2 + sub][:, :TS], ALU.mult))
            for dg in range(4):
                for kt in range(HK):
                    P.dma("sync", lambda kt=kt, dg=dg: nc.sync.dma_start(
                        out=wsl[:, :], in_=W2[kt * 128:(kt + 1) * 128, dg * 512:(dg + 1) * 512]))
                    for sub in range(4):
                        P.op("tensor", lambda kt=kt, sub=sub: nc.tensor.matmul(
                            pAs[sub][:, :TS], wsl[:, sub * 128:(sub + 1) * 128],
                            hm[:, kt * TS:(kt + 1) * TS], start=(kt == 0), stop=(kt == HK - 1)))
                for sub in range(4):
                    dt = dg * 4 + sub
                    P.op("vector", lambda dt=dt, sub=sub: nc.vector.tensor_tensor(
                        xr[:, dt * TS:(dt + 1) * TS], xb[:, dt * TS:(dt + 1) * TS],
                        pAs[sub][:, :TS], ALU.add))
            for kt in range(KT):
                P.op("vector", lambda kt=kt: nc.vector.tensor_copy(
                    xnb[:, kt * TS:(kt + 1) * TS], xr[:, kt * TS:(kt + 1) * TS]))
            emit_stats(P, nc, ln, src_bf, pD, pA0, pB1, pB2, sones_bf, scon, C_EPS5, mean=True)
            loadbuf(xb, xT, t0)
            for kt in range(KT):
                P.op("vector", lambda kt=kt: nc.vector.tensor_tensor(
                    xr[:, kt * TS:(kt + 1) * TS], xr[:, kt * TS:(kt + 1) * TS], pB1[:, :TS], ALU.mult))
                P.op("vector", lambda kt=kt: nc.vector.tensor_tensor(
                    xr[:, kt * TS:(kt + 1) * TS], xr[:, kt * TS:(kt + 1) * TS], pB2[:, :TS], ALU.subtract))
                P.op("vector", lambda kt=kt: nc.vector.tensor_tensor(
                    xr[:, kt * TS:(kt + 1) * TS], xr[:, kt * TS:(kt + 1) * TS],
                    xb[:, kt * TS:(kt + 1) * TS], ALU.add))
                P.dma("gpsimd", lambda kt=kt, t0=t0: nc.gpsimd.dma_start(
                    out=x1_o[kt * 128:(kt + 1) * 128, t0:t0 + TS], in_=xr[:, kt * TS:(kt + 1) * TS]))
        P.finalize(block, po)
    return nc



def build_L4(TLOC):
    nc = bass.Bass("TRN2", target_bir_lowering=False)
    consts, ones_bf, ones1, ident_bf = _std_tensors(nc)
    x_i = nc.dram_tensor("x_i", [D, TLOC], F32, kind="ExternalInput").ap()
    gT = nc.dram_tensor("gT", [D, 16384], BF16, kind="ExternalInput").ap()
    uT = nc.dram_tensor("uT", [D, 16384], BF16, kind="ExternalInput").ap()
    dT = nc.dram_tensor("dT", [16384, D], BF16, kind="ExternalInput").ap()
    x2_o = nc.dram_tensor("x2_o", [D, TLOC], F32, kind="ExternalOutput").ap()

    TS = min(512, TLOC)
    inter_s = nc.dram_tensor("inter_s", [16384, TS], BF16, kind="Internal").ap()
    NTC = TLOC // TS
    NH = 128

    with contextlib.ExitStack() as _es:
        xb = _es.enter_context(nc.sbuf_tensor("xb", [128, KT * TS], F32))
        xnb = _es.enter_context(nc.sbuf_tensor("xnb", [128, KT * TS], BF16))
        yb = _es.enter_context(nc.sbuf_tensor("yb", [128, KT * TS], F32))
        wsl = _es.enter_context(nc.sbuf_tensor("wsl", [128, 512], BF16))
        isl = _es.enter_context(nc.sbuf_tensor("isl", [128, 512], BF16))
        ev = _es.enter_context(nc.sbuf_tensor("ev", [128, TS], BF16))
        sones_bf = _es.enter_context(nc.sbuf_tensor("sones_bf", [128, 1], BF16))
        sones1 = _es.enter_context(nc.sbuf_tensor("sones1", [1, 128], F32))
        scon = _es.enter_context(nc.sbuf_tensor("scon", [128, NCONST], F32))
        ln = type("LN", (), {})()
        ln.TS = TS
        ln.tb = _es.enter_context(nc.sbuf_tensor("ln_tb_" + "d", [128, KT * TS], BF16))
        ln.srow = _es.enter_context(nc.sbuf_tensor("ln_sr_" + "d", [1, 6 * TS], F32))
        pA0 = _es.enter_context(nc.psum_tensor("pA0", [128, TS], F32))
        pA1 = _es.enter_context(nc.psum_tensor("pA1", [128, TS], F32))
        pA2 = _es.enter_context(nc.psum_tensor("pA2", [128, TS], F32))
        pA3 = _es.enter_context(nc.psum_tensor("pA3", [128, TS], F32))
        pB1 = _es.enter_context(nc.psum_tensor("pB1", [128, TS], F32))
        pB2 = _es.enter_context(nc.psum_tensor("pB2", [128, TS], F32))
        pD = _es.enter_context(nc.psum_tensor("pD", [128, TS], F32))
        po = _es.enter_context(nc.semaphore("po"))
        block = _es.enter_context(nc.Block())
        P = Prog(nc)
        P.ones1 = sones1
        pAs = [pA0, pA1, pA2, pA3]
        P.dma("sync", lambda: nc.sync.dma_start(out=scon[:, :], in_=consts[:, :]))
        P.dma("sync", lambda: nc.sync.dma_start(out=sones_bf[:, :], in_=ones_bf[:, :]))
        P.dma("sync", lambda: nc.sync.dma_start(out=sones1[:, :], in_=ones1[:, :]))

        for tc in range(NTC):
            t0 = tc * TS
            for kt in range(KT):
                P.dma("sync", lambda kt=kt, t0=t0: nc.sync.dma_start(
                    out=xb[:, kt * TS:(kt + 1) * TS], in_=x_i[kt * 128:(kt + 1) * 128, t0:t0 + TS]))
            for kt in range(KT):
                P.op("vector", lambda kt=kt: nc.vector.tensor_copy(
                    xnb[:, kt * TS:(kt + 1) * TS], xb[:, kt * TS:(kt + 1) * TS]))
            src_bf = [xnb[:, kt * TS:(kt + 1) * TS] for kt in range(KT)]
            emit_stats(P, nc, ln, src_bf, pD, pA0, pB1, pB2, sones_bf, scon, C_EPS5, mean=True)
            for kt in range(KT):
                P.op("vector", lambda kt=kt: nc.vector.scalar_tensor_tensor(
                    ln.tb[:, kt * TS:(kt + 1) * TS], xb[:, kt * TS:(kt + 1) * TS],
                    1.0, pB1[:, :TS], ALU.mult, ALU.mult))
            for kt in range(KT):
                P.op("vector", lambda kt=kt: nc.vector.tensor_tensor(
                    xnb[:, kt * TS:(kt + 1) * TS], ln.tb[:, kt * TS:(kt + 1) * TS],
                    pB2[:, :TS], ALU.subtract))
            for ng in range(NH // 2):
                for kt in range(KT):
                    P.dma("sync", lambda kt=kt, ng=ng: nc.sync.dma_start(
                        out=wsl[:, 0:256], in_=gT[kt * 128:(kt + 1) * 128, ng * 256:(ng + 1) * 256]))
                    for sub in range(2):
                        P.op("tensor", lambda kt=kt, sub=sub: nc.tensor.matmul(
                            pAs[sub][:, :TS], wsl[:, sub * 128:(sub + 1) * 128],
                            xnb[:, kt * TS:(kt + 1) * TS], start=(kt == 0), stop=(kt == KT - 1)))
                for kt in range(KT):
                    P.dma("sync", lambda kt=kt, ng=ng: nc.sync.dma_start(
                        out=wsl[:, 0:256], in_=uT[kt * 128:(kt + 1) * 128, ng * 256:(ng + 1) * 256]))
                    for sub in range(2):
                        P.op("tensor", lambda kt=kt, sub=sub: nc.tensor.matmul(
                            pAs[2 + sub][:, :TS], wsl[:, sub * 128:(sub + 1) * 128],
                            xnb[:, kt * TS:(kt + 1) * TS], start=(kt == 0), stop=(kt == KT - 1)))
                for sub in range(2):
                    nt = ng * 2 + sub
                    P.op("scalar", lambda sub=sub: nc.scalar.activation(
                        ev[:, :TS], pAs[sub][:, :TS], AF.Gelu_apprx_tanh))
                    P.op("vector", lambda sub=sub: nc.vector.tensor_tensor(
                        ev[:, :TS], ev[:, :TS], pAs[2 + sub][:, :TS], ALU.mult))
                    P.dma("gpsimd", lambda nt=nt: nc.gpsimd.dma_start(
                        out=inter_s[nt * 128:(nt + 1) * 128, 0:TS], in_=ev[:, :TS]))
            for dg in range(4):
                for kt in range(NH):
                    P.dma("sync", lambda kt=kt: nc.sync.dma_start(
                        out=isl[:, :TS], in_=inter_s[kt * 128:(kt + 1) * 128, 0:TS]))
                    P.dma("sync", lambda kt=kt, dg=dg: nc.sync.dma_start(
                        out=wsl[:, :], in_=dT[kt * 128:(kt + 1) * 128, dg * 512:(dg + 1) * 512]))
                    for sub in range(4):
                        P.op("tensor", lambda kt=kt, sub=sub: nc.tensor.matmul(
                            pAs[sub][:, :TS], wsl[:, sub * 128:(sub + 1) * 128],
                            isl[:, :TS], start=(kt == 0), stop=(kt == NH - 1)))
                for sub in range(4):
                    dt = dg * 4 + sub
                    P.op("vector", lambda dt=dt, sub=sub: nc.vector.tensor_copy(
                        yb[:, dt * TS:(dt + 1) * TS], pAs[sub][:, :TS]))
            for kt in range(KT):
                P.op("vector", lambda kt=kt: nc.vector.tensor_copy(
                    xnb[:, kt * TS:(kt + 1) * TS], yb[:, kt * TS:(kt + 1) * TS]))
            emit_stats(P, nc, ln, src_bf, pD, pA0, pB1, pB2, sones_bf, scon, C_EPS5, mean=True)
            for kt in range(KT):
                P.op("vector", lambda kt=kt: nc.vector.tensor_tensor(
                    yb[:, kt * TS:(kt + 1) * TS], yb[:, kt * TS:(kt + 1) * TS], pB1[:, :TS], ALU.mult))
                P.op("vector", lambda kt=kt: nc.vector.tensor_tensor(
                    yb[:, kt * TS:(kt + 1) * TS], yb[:, kt * TS:(kt + 1) * TS], pB2[:, :TS], ALU.subtract))
                P.op("vector", lambda kt=kt: nc.vector.tensor_tensor(
                    yb[:, kt * TS:(kt + 1) * TS], yb[:, kt * TS:(kt + 1) * TS],
                    xb[:, kt * TS:(kt + 1) * TS], ALU.add))
                P.dma("gpsimd", lambda kt=kt, t0=t0: nc.gpsimd.dma_start(
                    out=x2_o[kt * 128:(kt + 1) * 128, t0:t0 + TS], in_=yb[:, kt * TS:(kt + 1) * TS]))
        P.finalize(block, po)
    return nc



def build_L5(TLOC):
    nc = bass.Bass("TRN2", target_bir_lowering=False)
    consts, ones_bf, ones1, ident_bf = _std_tensors(nc)
    x_i = nc.dram_tensor("x_i", [D, TLOC], F32, kind="ExternalInput").ap()
    WqkT = nc.dram_tensor("WqkT", [D, 8192], BF16, kind="ExternalInput").ap()
    q_o = nc.dram_tensor("q_o", [D, TLOC], BF16, kind="ExternalOutput").ap()
    k_o = nc.dram_tensor("k_o", [D, TLOC], BF16, kind="ExternalOutput").ap()
    v_o = nc.dram_tensor("v_o", [D, TLOC], BF16, kind="ExternalOutput").ap()
    u_o = nc.dram_tensor("u_o", [D, TLOC], BF16, kind="ExternalOutput").ap()

    TS = min(512, TLOC)
    NTC = TLOC // TS

    with contextlib.ExitStack() as _es:
        xb = _es.enter_context(nc.sbuf_tensor("xb", [128, KT * TS], F32))
        xnb = _es.enter_context(nc.sbuf_tensor("xnb", [128, KT * TS], BF16))
        wsl = _es.enter_context(nc.sbuf_tensor("wsl", [128, 512], BF16))
        ev = _es.enter_context(nc.sbuf_tensor("ev", [128, TS], BF16))
        sones_bf = _es.enter_context(nc.sbuf_tensor("sones_bf", [128, 1], BF16))
        sones1 = _es.enter_context(nc.sbuf_tensor("sones1", [1, 128], F32))
        scon = _es.enter_context(nc.sbuf_tensor("scon", [128, NCONST], F32))
        ln = type("LN", (), {})()
        ln.TS = TS
        ln.tb = _es.enter_context(nc.sbuf_tensor("ln_tb_" + "e", [128, KT * TS], BF16))
        ln.srow = _es.enter_context(nc.sbuf_tensor("ln_sr_" + "e", [1, 6 * TS], F32))
        pA0 = _es.enter_context(nc.psum_tensor("pA0", [128, TS], F32))
        pA1 = _es.enter_context(nc.psum_tensor("pA1", [128, TS], F32))
        pA2 = _es.enter_context(nc.psum_tensor("pA2", [128, TS], F32))
        pA3 = _es.enter_context(nc.psum_tensor("pA3", [128, TS], F32))
        pB1 = _es.enter_context(nc.psum_tensor("pB1", [128, TS], F32))
        pB2 = _es.enter_context(nc.psum_tensor("pB2", [128, TS], F32))
        pD = _es.enter_context(nc.psum_tensor("pD", [128, TS], F32))
        po = _es.enter_context(nc.semaphore("po"))
        block = _es.enter_context(nc.Block())
        P = Prog(nc)
        P.ones1 = sones1
        pAs = [pA0, pA1, pA2, pA3]
        P.dma("sync", lambda: nc.sync.dma_start(out=scon[:, :], in_=consts[:, :]))
        P.dma("sync", lambda: nc.sync.dma_start(out=sones_bf[:, :], in_=ones_bf[:, :]))
        P.dma("sync", lambda: nc.sync.dma_start(out=sones1[:, :], in_=ones1[:, :]))

        outs = [q_o, k_o, v_o, u_o]
        for tc in range(NTC):
            t0 = tc * TS
            for kt in range(KT):
                P.dma("sync", lambda kt=kt, t0=t0: nc.sync.dma_start(
                    out=xb[:, kt * TS:(kt + 1) * TS], in_=x_i[kt * 128:(kt + 1) * 128, t0:t0 + TS]))
            for kt in range(KT):
                P.op("vector", lambda kt=kt: nc.vector.tensor_copy(
                    xnb[:, kt * TS:(kt + 1) * TS], xb[:, kt * TS:(kt + 1) * TS]))
            src_bf = [xnb[:, kt * TS:(kt + 1) * TS] for kt in range(KT)]
            emit_stats(P, nc, ln, src_bf, pD, pA0, pB1, pB2, sones_bf, scon, C_EPS5, mean=True)
            for kt in range(KT):
                P.op("vector", lambda kt=kt: nc.vector.tensor_tensor(
                    xb[:, kt * TS:(kt + 1) * TS], xb[:, kt * TS:(kt + 1) * TS], pB1[:, :TS], ALU.mult))
            for kt in range(KT):
                P.op("vector", lambda kt=kt: nc.vector.tensor_tensor(
                    xb[:, kt * TS:(kt + 1) * TS], xb[:, kt * TS:(kt + 1) * TS], pB2[:, :TS], ALU.subtract))
            for kt in range(KT):
                P.op("vector", lambda kt=kt: nc.vector.tensor_copy(
                    xnb[:, kt * TS:(kt + 1) * TS], xb[:, kt * TS:(kt + 1) * TS]))
            for part in range(4):
                silu = part < 2
                for ns in range(4):
                    for kt in range(KT):
                        P.dma("sync", lambda kt=kt, ns=ns, part=part: nc.sync.dma_start(
                            out=wsl[:, :],
                            in_=WqkT[kt * 128:(kt + 1) * 128,
                                     part * 2048 + ns * 512:part * 2048 + (ns + 1) * 512]))
                        for sub in range(4):
                            P.op("tensor", lambda kt=kt, sub=sub: nc.tensor.matmul(
                                pAs[sub][:, :TS], wsl[:, sub * 128:(sub + 1) * 128],
                                xnb[:, kt * TS:(kt + 1) * TS], start=(kt == 0), stop=(kt == KT - 1)))
                    for sub in range(4):
                        nt = ns * 4 + sub
                        if silu:
                            P.op("scalar", lambda sub=sub: nc.scalar.activation(
                                ev[:, :TS], pAs[sub][:, :TS], AF.Silu))
                        else:
                            P.op("vector", lambda sub=sub: nc.vector.tensor_copy(
                                ev[:, :TS], pAs[sub][:, :TS]))
                        P.dma("gpsimd", lambda nt=nt, t0=t0, part=part: nc.gpsimd.dma_start(
                            out=outs[part][nt * 128:(nt + 1) * 128, t0:t0 + TS], in_=ev[:, :TS]))
        P.finalize(block, po)
    return nc



def build_L6(NSEQ):
    nc = bass.Bass("TRN2", target_bir_lowering=False)
    consts, ones_bf, ones1, ident_bf = _std_tensors(nc)
    delta = nc.dram_tensor("delta", [128, 128], F32, kind="ExternalInput").ap()
    iota_f = nc.dram_tensor("iota_f", [1, 128], F32, kind="ExternalInput").ap()
    iota_r = nc.dram_tensor("iota_r", [1, 128], F32, kind="ExternalInput").ap()
    slope = nc.dram_tensor("slope", [4, 128], F32, kind="ExternalInput").ap()
    q_i = nc.dram_tensor("q_i", [4, 128, NSEQ], BF16, kind="ExternalInput").ap()
    k_i = nc.dram_tensor("k_i", [4, 128, NSEQ], BF16, kind="ExternalInput").ap()
    v_i = nc.dram_tensor("v_i", [4, 128, NSEQ], BF16, kind="ExternalInput").ap()
    o_o = nc.dram_tensor("o_o", [4, NSEQ, 128], BF16, kind="ExternalOutput").ap()

    NCH = NSEQ // 128

    with contextlib.ExitStack() as _es:
        sq = _es.enter_context(nc.sbuf_tensor("sq", [128, 128], BF16))
        sk = _es.enter_context(nc.sbuf_tensor("sk", [128, 128], BF16))
        sv = _es.enter_context(nc.sbuf_tensor("sv", [128, 128], BF16))
        sqh = _es.enter_context(nc.sbuf_tensor("sqh", [128, 128], BF16))
        skh = _es.enter_context(nc.sbuf_tensor("skh", [128, 128], BF16))
        scm = _es.enter_context(nc.sbuf_tensor("scm", [128, 128], BF16))
        svt = _es.enter_context(nc.sbuf_tensor("svt", [128, 128], BF16))
        skht = _es.enter_context(nc.sbuf_tensor("skht", [128, 128], BF16))
        sS = _es.enter_context(nc.sbuf_tensor("sS", [128, 128], F32))
        sSb = _es.enter_context(nc.sbuf_tensor("sSb", [128, 128], BF16))
        sD = _es.enter_context(nc.sbuf_tensor("sD", [128, 128], F32))
        sqd = _es.enter_context(nc.sbuf_tensor("sqd", [128, 128], BF16))
        skd = _es.enter_context(nc.sbuf_tensor("skd", [128, 128], BF16))
        srow = _es.enter_context(nc.sbuf_tensor("srow", [1, 128], F32))
        sbd = _es.enter_context(nc.sbuf_tensor("sbd", [128, 1], F32))
        sscol = _es.enter_context(nc.sbuf_tensor("sscol", [128, 1], F32))
        sdel = _es.enter_context(nc.sbuf_tensor("sdel", [128, 128], F32))
        siof = _es.enter_context(nc.sbuf_tensor("siof", [1, 128], F32))
        sior = _es.enter_context(nc.sbuf_tensor("sior", [1, 128], F32))
        son = _es.enter_context(nc.sbuf_tensor("son", [128, 128], BF16))
        sid = _es.enter_context(nc.sbuf_tensor("sid", [128, 128], BF16))
        sones1 = _es.enter_context(nc.sbuf_tensor("sones1", [1, 128], F32))
        scon = _es.enter_context(nc.sbuf_tensor("scon", [128, NCONST], F32))
        psc = _es.enter_context(nc.psum_tensor("psc", [128, 128], F32))
        pvt = _es.enter_context(nc.psum_tensor("pvt", [128, 128], BF16))
        po_ = _es.enter_context(nc.psum_tensor("po_", [128, 128], F32))
        pdS = _es.enter_context(nc.psum_tensor("pdS", [128, 128], F32))
        pbc = _es.enter_context(nc.psum_tensor("pbc", [128, 128], F32))
        po = _es.enter_context(nc.semaphore("po"))
        block = _es.enter_context(nc.Block())
        P = Prog(nc)
        P.dma("sync", lambda: nc.sync.dma_start(out=scon[:, :], in_=consts[:, :]))
        P.dma("sync", lambda: nc.sync.dma_start(out=sdel[:, :], in_=delta[:, :]))
        P.dma("sync", lambda: nc.sync.dma_start(out=siof[:, :], in_=iota_f[:, :]))
        P.dma("sync", lambda: nc.sync.dma_start(out=sior[:, :], in_=iota_r[:, :]))
        P.dma("sync", lambda: nc.sync.dma_start(out=sid[:, :], in_=ident_bf[:, :]))
        P.dma("sync", lambda: nc.sync.dma_start(out=sones1[:, :], in_=ones1[:, :]))

        for st in range(4):
            P.dma("sync", lambda st=st: nc.sync.dma_start(
                out=sscol[:, 0:1], in_=slope[st, :].rearrange("(p o) -> p o", o=1)))
            P.dma("sync", lambda st=st: nc.sync.dma_start(
                out=srow[:, :], in_=slope[st:st + 1, :]))
            P.op("vector", lambda: nc.vector.tensor_scalar(
                sD[:, :], sdel[:, :], sscol[:, 0:1], None, ALU.mult))
            P.op("scalar", lambda: nc.scalar.activation(
                sD[:, :], sD[:, :], AF.Exp, scale=scon[:, C_NEG1:C_NEG1 + 1]))
            P.op("vector", lambda: nc.vector.tensor_mul(srow[:, :], srow[:, :], siof[:, :]))
            P.op("scalar", lambda: nc.scalar.activation(
                srow[:, :], srow[:, :], AF.Exp, scale=scon[0:1, C_NEG1:C_NEG1 + 1]))
            P.op("tensor", lambda: nc.tensor.matmul(
                pbc[:, :], sones1[:, :], srow[:, :], start=True, stop=True))
            P.op("vector", lambda: nc.vector.tensor_copy(sqd[:, :], pbc[:, :]))
            P.dma("sync", lambda st=st: nc.sync.dma_start(
                out=srow[:, :], in_=slope[st:st + 1, :]))
            P.op("vector", lambda: nc.vector.tensor_mul(srow[:, :], srow[:, :], sior[:, :]))
            P.op("scalar", lambda: nc.scalar.activation(
                srow[:, :], srow[:, :], AF.Exp, scale=scon[0:1, C_NEG1:C_NEG1 + 1]))
            P.op("tensor", lambda: nc.tensor.matmul(
                pbc[:, :], sones1[:, :], srow[:, :], start=True, stop=True))
            P.op("vector", lambda: nc.vector.tensor_copy(skd[:, :], pbc[:, :]))
            P.op("scalar", lambda: nc.scalar.activation(
                sbd[:, 0:1], sscol[:, 0:1], AF.Exp, scale=scon[:, C_NEG128:C_NEG128 + 1]))
            P.op("vector", lambda: nc.vector.memset(sS[:, :], 0.0))

            for ch in range(NCH):
                c0 = ch * 128
                P.dma("sync", lambda st=st, c0=c0: nc.sync.dma_start(
                    out=sq[:, :], in_=q_i[st, :, c0:c0 + 128]))
                P.dma("sync", lambda st=st, c0=c0: nc.sync.dma_start(
                    out=sk[:, :], in_=k_i[st, :, c0:c0 + 128]))
                P.dma("sync", lambda st=st, c0=c0: nc.sync.dma_start(
                    out=sv[:, :], in_=v_i[st, :, c0:c0 + 128]))
                P.op("vector", lambda: nc.vector.tensor_mul(sqh[:, :], sq[:, :], sqd[:, :]))
                P.op("vector", lambda: nc.vector.tensor_mul(skh[:, :], sk[:, :], skd[:, :]))
                P.op("vector", lambda: nc.vector.tensor_copy(sSb[:, :], sS[:, :]))
                P.op("tensor", lambda: nc.tensor.matmul(
                    psc[:, :], sk[:, :], sq[:, :], start=True, stop=True))
                P.op("vector", lambda: nc.vector.tensor_tensor(
                    scm[:, :], psc[:, :], sD[:, :], ALU.mult))
                P.op("tensor", lambda: nc.tensor.transpose(pvt[:, :], sv[:, :], sid[:, :]))
                P.op("vector", lambda: nc.vector.tensor_copy(svt[:, :], pvt[:, :]))
                P.op("tensor", lambda: nc.tensor.matmul(
                    po_[:, :], scm[:, :], svt[:, :], start=True, stop=False))
                P.op("tensor", lambda: nc.tensor.matmul(
                    po_[:, :], sqh[:, :], sSb[:, :], start=False, stop=True))
                P.op("vector", lambda: nc.vector.tensor_copy(son[:, :], po_[:, :]))
                P.dma("gpsimd", lambda st=st, c0=c0: nc.gpsimd.dma_start(
                    out=o_o[st, c0:c0 + 128, :], in_=son[:, :]))
                P.op("tensor", lambda: nc.tensor.transpose(pvt[:, :], skh[:, :], sid[:, :]))
                P.op("vector", lambda: nc.vector.tensor_copy(skht[:, :], pvt[:, :]))
                P.op("tensor", lambda: nc.tensor.matmul(
                    pdS[:, :], skht[:, :], svt[:, :], start=True, stop=True))
                P.op("vector", lambda: nc.vector.tensor_scalar(
                    sS[:, :], sS[:, :], sbd[:, 0:1], None, ALU.mult))
                P.op("vector", lambda: nc.vector.tensor_tensor(
                    sS[:, :], sS[:, :], pdS[:, :], ALU.add))
        P.finalize(block, po)
    return nc



def build_L7(TLOC):
    nc = bass.Bass("TRN2", target_bir_lowering=False)
    consts, ones_bf, ones1, ident_bf = _std_tensors(nc)
    x_i = nc.dram_tensor("x_i", [D, TLOC], F32, kind="ExternalInput").ap()
    o_i = nc.dram_tensor("o_i", [D, TLOC], BF16, kind="ExternalInput").ap()
    u_i = nc.dram_tensor("u_i", [D, TLOC], BF16, kind="ExternalInput").ap()
    WoT = nc.dram_tensor("WoT", [D, D], BF16, kind="ExternalInput").ap()
    WgT = nc.dram_tensor("WgT", [D, 8192], BF16, kind="ExternalInput").ap()
    WuT = nc.dram_tensor("WuT", [D, 8192], BF16, kind="ExternalInput").ap()
    WdT = nc.dram_tensor("WdT", [8192, D], BF16, kind="ExternalInput").ap()
    x4_o = nc.dram_tensor("x4_o", [D, TLOC], F32, kind="ExternalOutput").ap()

    TS = min(256, TLOC)
    NTC = TLOC // TS
    HK = 64

    with contextlib.ExitStack() as _es:
        xb = _es.enter_context(nc.sbuf_tensor("xb", [128, KT * TS], F32))
        ab = _es.enter_context(nc.sbuf_tensor("ab", [128, KT * TS], F32))
        ob = _es.enter_context(nc.sbuf_tensor("ob", [128, KT * TS], BF16))
        ub = _es.enter_context(nc.sbuf_tensor("ub", [128, KT * TS], BF16))
        xnb = _es.enter_context(nc.sbuf_tensor("xnb", [128, KT * TS], BF16))
        hm = _es.enter_context(nc.sbuf_tensor("hm", [128, HK * TS], BF16))
        wsl = _es.enter_context(nc.sbuf_tensor("wsl", [128, 512], BF16))
        ev = _es.enter_context(nc.sbuf_tensor("ev", [128, TS], BF16))
        sones_bf = _es.enter_context(nc.sbuf_tensor("sones_bf", [128, 1], BF16))
        sones1 = _es.enter_context(nc.sbuf_tensor("sones1", [1, 128], F32))
        scon = _es.enter_context(nc.sbuf_tensor("scon", [128, NCONST], F32))
        ln = type("LN", (), {})()
        ln.TS = TS
        ln.tb = _es.enter_context(nc.sbuf_tensor("ln_tb_" + "g", [128, KT * TS], BF16))
        ln.srow = _es.enter_context(nc.sbuf_tensor("ln_sr_" + "g", [1, 6 * TS], F32))
        pA0 = _es.enter_context(nc.psum_tensor("pA0", [128, TS], F32))
        pA1 = _es.enter_context(nc.psum_tensor("pA1", [128, TS], F32))
        pA2 = _es.enter_context(nc.psum_tensor("pA2", [128, TS], F32))
        pA3 = _es.enter_context(nc.psum_tensor("pA3", [128, TS], F32))
        pB1 = _es.enter_context(nc.psum_tensor("pB1", [128, TS], F32))
        pB2 = _es.enter_context(nc.psum_tensor("pB2", [128, TS], F32))
        pD = _es.enter_context(nc.psum_tensor("pD", [128, TS], F32))
        po = _es.enter_context(nc.semaphore("po"))
        block = _es.enter_context(nc.Block())
        P = Prog(nc)
        P.ones1 = sones1
        pAs = [pA0, pA1, pA2, pA3]
        P.dma("sync", lambda: nc.sync.dma_start(out=scon[:, :], in_=consts[:, :]))
        P.dma("sync", lambda: nc.sync.dma_start(out=sones_bf[:, :], in_=ones_bf[:, :]))
        P.dma("sync", lambda: nc.sync.dma_start(out=sones1[:, :], in_=ones1[:, :]))

        ob_t = [None] * KT
        xn_t = [None] * KT
        for kt in range(KT):
            ob_t[kt] = ob[:, kt * TS:(kt + 1) * TS]
            xn_t[kt] = xnb[:, kt * TS:(kt + 1) * TS]

        def loadbuf(buf, srcp, t0):
            for kt in range(KT):
                P.dma("sync", lambda kt=kt, buf=buf, srcp=srcp, t0=t0: nc.sync.dma_start(
                    out=buf[:, kt * TS:(kt + 1) * TS], in_=srcp[kt * 128:(kt + 1) * 128, t0:t0 + TS]))

        for tc in range(NTC):
            t0 = tc * TS
            loadbuf(xb, x_i, t0)
            loadbuf(ob, o_i, t0)
            loadbuf(ub, u_i, t0)
            emit_stats(P, nc, ln, ob_t, pD, pA0, pB1, pB2, sones_bf, scon, C_EPS6, mean=False)
            for kt in range(KT):
                P.op("vector", lambda kt=kt: nc.vector.tensor_tensor(
                    ob[:, kt * TS:(kt + 1) * TS], ob[:, kt * TS:(kt + 1) * TS], pB1[:, :TS], ALU.mult))
            for kt in range(KT):
                P.op("vector", lambda kt=kt: nc.vector.tensor_mul(
                    ob[:, kt * TS:(kt + 1) * TS], ob[:, kt * TS:(kt + 1) * TS],
                    ub[:, kt * TS:(kt + 1) * TS]))
            for ns in range(4):
                for kt in range(KT):
                    P.dma("sync", lambda kt=kt, ns=ns: nc.sync.dma_start(
                        out=wsl[:, :], in_=WoT[kt * 128:(kt + 1) * 128, ns * 512:(ns + 1) * 512]))
                    for sub in range(4):
                        P.op("tensor", lambda kt=kt, sub=sub: nc.tensor.matmul(
                            pAs[sub][:, :TS], wsl[:, sub * 128:(sub + 1) * 128],
                            ob[:, kt * TS:(kt + 1) * TS], start=(kt == 0), stop=(kt == KT - 1)))
                for sub in range(4):
                    nt = ns * 4 + sub
                    P.op("vector", lambda nt=nt, sub=sub: nc.vector.tensor_copy(
                        ab[:, nt * TS:(nt + 1) * TS], pAs[sub][:, :TS]))
            for kt in range(KT):
                P.op("vector", lambda kt=kt: nc.vector.tensor_copy(
                    xnb[:, kt * TS:(kt + 1) * TS], ab[:, kt * TS:(kt + 1) * TS]))
            emit_stats(P, nc, ln, xn_t, pD, pA0, pB1, pB2, sones_bf, scon, C_EPS5, mean=True)
            for kt in range(KT):
                P.op("vector", lambda kt=kt: nc.vector.tensor_tensor(
                    ab[:, kt * TS:(kt + 1) * TS], ab[:, kt * TS:(kt + 1) * TS], pB1[:, :TS], ALU.mult))
                P.op("vector", lambda kt=kt: nc.vector.tensor_tensor(
                    ab[:, kt * TS:(kt + 1) * TS], ab[:, kt * TS:(kt + 1) * TS], pB2[:, :TS], ALU.subtract))
                P.op("vector", lambda kt=kt: nc.vector.tensor_tensor(
                    xb[:, kt * TS:(kt + 1) * TS], xb[:, kt * TS:(kt + 1) * TS],
                    ab[:, kt * TS:(kt + 1) * TS], ALU.add))
            for kt in range(KT):
                P.op("vector", lambda kt=kt: nc.vector.tensor_copy(
                    xnb[:, kt * TS:(kt + 1) * TS], xb[:, kt * TS:(kt + 1) * TS]))
            emit_stats(P, nc, ln, xn_t, pD, pA0, pB1, pB2, sones_bf, scon, C_EPS5, mean=True)
            for kt in range(KT):
                P.op("vector", lambda kt=kt: nc.vector.scalar_tensor_tensor(
                    ab[:, kt * TS:(kt + 1) * TS], xb[:, kt * TS:(kt + 1) * TS],
                    1.0, pB1[:, :TS], ALU.mult, ALU.mult))
            for kt in range(KT):
                P.op("vector", lambda kt=kt: nc.vector.tensor_tensor(
                    xnb[:, kt * TS:(kt + 1) * TS], ab[:, kt * TS:(kt + 1) * TS],
                    pB2[:, :TS], ALU.subtract))
            for ns in range(32):
                for kt in range(KT):
                    P.dma("sync", lambda kt=kt, ns=ns: nc.sync.dma_start(
                        out=wsl[:, 0:256], in_=WgT[kt * 128:(kt + 1) * 128, ns * 256:(ns + 1) * 256]))
                    for sub in range(2):
                        P.op("tensor", lambda kt=kt, sub=sub: nc.tensor.matmul(
                            pAs[sub][:, :TS], wsl[:, sub * 128:(sub + 1) * 128],
                            xnb[:, kt * TS:(kt + 1) * TS], start=(kt == 0), stop=(kt == KT - 1)))
                for kt in range(KT):
                    P.dma("sync", lambda kt=kt, ns=ns: nc.sync.dma_start(
                        out=wsl[:, 0:256], in_=WuT[kt * 128:(kt + 1) * 128, ns * 256:(ns + 1) * 256]))
                    for sub in range(2):
                        P.op("tensor", lambda kt=kt, sub=sub: nc.tensor.matmul(
                            pAs[2 + sub][:, :TS], wsl[:, sub * 128:(sub + 1) * 128],
                            xnb[:, kt * TS:(kt + 1) * TS], start=(kt == 0), stop=(kt == KT - 1)))
                for sub in range(2):
                    ht = ns * 2 + sub
                    P.op("scalar", lambda sub=sub: nc.scalar.activation(
                        ev[:, :TS], pAs[sub][:, :TS], AF.Silu))
                    P.op("vector", lambda ht=ht, sub=sub: nc.vector.tensor_tensor(
                        hm[:, ht * TS:(ht + 1) * TS], ev[:, :TS], pAs[2 + sub][:, :TS], ALU.mult))
            for dg in range(4):
                for kt in range(HK):
                    P.dma("sync", lambda kt=kt, dg=dg: nc.sync.dma_start(
                        out=wsl[:, :], in_=WdT[kt * 128:(kt + 1) * 128, dg * 512:(dg + 1) * 512]))
                    for sub in range(4):
                        P.op("tensor", lambda kt=kt, sub=sub: nc.tensor.matmul(
                            pAs[sub][:, :TS], wsl[:, sub * 128:(sub + 1) * 128],
                            hm[:, kt * TS:(kt + 1) * TS], start=(kt == 0), stop=(kt == HK - 1)))
                for sub in range(4):
                    dt = dg * 4 + sub
                    P.op("vector", lambda dt=dt, sub=sub: nc.vector.tensor_copy(
                        ab[:, dt * TS:(dt + 1) * TS], pAs[sub][:, :TS]))
            for kt in range(KT):
                P.op("vector", lambda kt=kt: nc.vector.tensor_copy(
                    xnb[:, kt * TS:(kt + 1) * TS], ab[:, kt * TS:(kt + 1) * TS]))
            emit_stats(P, nc, ln, xn_t, pD, pA0, pB1, pB2, sones_bf, scon, C_EPS5, mean=True)
            for kt in range(KT):
                P.op("vector", lambda kt=kt: nc.vector.tensor_tensor(
                    ab[:, kt * TS:(kt + 1) * TS], ab[:, kt * TS:(kt + 1) * TS], pB1[:, :TS], ALU.mult))
                P.op("vector", lambda kt=kt: nc.vector.tensor_tensor(
                    ab[:, kt * TS:(kt + 1) * TS], ab[:, kt * TS:(kt + 1) * TS], pB2[:, :TS], ALU.subtract))
                P.op("vector", lambda kt=kt: nc.vector.tensor_tensor(
                    ab[:, kt * TS:(kt + 1) * TS], ab[:, kt * TS:(kt + 1) * TS],
                    xb[:, kt * TS:(kt + 1) * TS], ALU.add))
                P.dma("gpsimd", lambda kt=kt, t0=t0: nc.gpsimd.dma_start(
                    out=x4_o[kt * 128:(kt + 1) * 128, t0:t0 + TS], in_=ab[:, kt * TS:(kt + 1) * TS]))
        P.finalize(block, po)
    return nc



DENSE_SPECS = [
    ("Wq", (2048, 1024)), ("Wk", (2048, 1024)), ("Wkg1", (2048, 16)),
    ("Wkg2", (16, 1024)), ("Wv", (2048, 2048)), ("Wg", (2048, 2048)),
    ("Wo", (2048, 2048)), ("W1", (2048, 5632)), ("W3", (2048, 5632)),
    ("W2", (5632, 2048)), ("qkvuT", (2048, 8192)), ("outT", (2048, 2048)),
    ("gT1", (2048, 8192)), ("uT1", (2048, 8192)), ("dT1", (8192, 2048)),
]


def as_u8(x_i32):
    b = np.ascontiguousarray(x_i32).view(np.uint8)
    return np.ascontiguousarray(b.reshape(x_i32.shape[0], x_i32.shape[1], 4)[:, :, 0])


def run(nc, in_maps, n=NCORE):
    return run_bass_kernel_spmd(nc, in_maps, core_ids=list(range(n))).results


def kernel(L0_attn_x, L0_attn_Wq, L0_attn_Wk, L0_attn_Wkg1, L0_attn_Wkg2,
           L0_attn_bkg2, L0_attn_Wv, L0_attn_Wg, L0_attn_bg, L0_attn_Wo,
           L0_attn_ln1_w, L0_attn_ln2_w, L0_attn_W1, L0_attn_W3, L0_attn_W2,
           L0_ffn_gate_W_packed, L0_ffn_gate_scales, L0_ffn_up_W_packed,
           L0_ffn_up_scales, L0_ffn_down_W_packed, L0_ffn_down_scales,
           L1_attn_slope_rate, L1_attn_W_qkvu, L1_attn_W_out,
           L1_ffn_w_gate, L1_ffn_w_up, L1_ffn_w_down):
    B, L, _ = L0_attn_x.shape
    TLOC = (B * L) // NCORE
    NQ = L // TLOC
    x = np.asarray(L0_attn_x, dtype=np.float32)

    std = make_std_arrays()

    def core_bt(c):
        return c // NQ, (c % NQ) * TLOC

    dense = dict(
        Wq=np.asarray(L0_attn_Wq, np.float32), Wk=np.asarray(L0_attn_Wk, np.float32),
        Wkg1=np.asarray(L0_attn_Wkg1, np.float32), Wkg2=np.asarray(L0_attn_Wkg2, np.float32),
        Wv=np.asarray(L0_attn_Wv, np.float32), Wg=np.asarray(L0_attn_Wg, np.float32),
        Wo=np.asarray(L0_attn_Wo, np.float32), W1=np.asarray(L0_attn_W1, np.float32),
        W3=np.asarray(L0_attn_W3, np.float32), W2=np.asarray(L0_attn_W2, np.float32),
        qkvuT=np.ascontiguousarray(np.asarray(L1_attn_W_qkvu, np.float32).T),
        outT=np.ascontiguousarray(np.asarray(L1_attn_W_out, np.float32).T),
        gT1=np.ascontiguousarray(np.asarray(L1_ffn_w_gate, np.float32).T),
        uT1=np.ascontiguousarray(np.asarray(L1_ffn_w_up, np.float32).T),
        dT1=np.ascontiguousarray(np.asarray(L1_ffn_w_down, np.float32).T),
    )
    flat = np.concatenate([dense[n].ravel() for n, _ in DENSE_SPECS])
    Ltot = flat.size
    unit = NCORE * 128 * 8192
    pad = (-Ltot) % unit
    flat = np.concatenate([flat, np.zeros(pad, np.float32)])
    FW = flat.size // (NCORE * 128)
    flat_parts = flat.reshape(NCORE, 128, FW)

    g_u8 = as_u8(np.asarray(L0_ffn_gate_W_packed))
    u_u8 = as_u8(np.asarray(L0_ffn_up_W_packed))
    d_u8 = as_u8(np.asarray(L0_ffn_down_W_packed))
    g_sc = np.asarray(L0_ffn_gate_scales, np.float32)
    u_sc = np.asarray(L0_ffn_up_scales, np.float32)
    d_sc = np.asarray(L0_ffn_down_scales, np.float32)

    ncW = build_W(FW)
    maps = []
    for c in range(NCORE):
        maps.append(dict(
            flat_in=np.ascontiguousarray(flat_parts[c]),
            g_pk=np.ascontiguousarray(g_u8[c * 2048:(c + 1) * 2048]),
            g_sc=np.ascontiguousarray(g_sc[c * 2048:(c + 1) * 2048]),
            u_pk=np.ascontiguousarray(u_u8[c * 2048:(c + 1) * 2048]),
            u_sc=np.ascontiguousarray(u_sc[c * 2048:(c + 1) * 2048]),
            d_pk=np.ascontiguousarray(d_u8[c * 256:(c + 1) * 256]),
            d_sc=np.ascontiguousarray(d_sc[c * 256:(c + 1) * 256]),
            **std))
    rW = run(ncW, maps)
    flat_bf = np.concatenate([np.asarray(r["flat_out"]).ravel() for r in rW])[:Ltot]
    wbf = {}
    off = 0
    for name, shp in DENSE_SPECS:
        n = shp[0] * shp[1]
        wbf[name] = np.ascontiguousarray(flat_bf[off:off + n].reshape(shp))
        off += n
    gT = np.concatenate([np.asarray(r["gT"]) for r in rW], axis=1)
    uT = np.concatenate([np.asarray(r["uT"]) for r in rW], axis=1)
    dT = np.concatenate([np.asarray(r["dT"]) for r in rW], axis=1)

    ln1c = np.ascontiguousarray(np.asarray(L0_attn_ln1_w, np.float32).reshape(16, 128).T)
    ln2c = np.ascontiguousarray(np.asarray(L0_attn_ln2_w, np.float32).reshape(16, 128).T)
    bgc = np.ascontiguousarray(np.asarray(L0_attn_bg, np.float32).reshape(16, 128).T)
    bkgc = np.ascontiguousarray(np.asarray(L0_attn_bkg2, np.float32).reshape(8, 128).T)

    nc1 = build_L1(TLOC)
    maps = []
    xTs = []
    for c in range(NCORE):
        b, t0 = core_bt(c)
        xT = np.ascontiguousarray(x[b, t0:t0 + TLOC, :].T)
        xTs.append(xT)
        maps.append(dict(xT=xT, Wq=wbf["Wq"], Wk=wbf["Wk"], Wkg1=wbf["Wkg1"],
                         Wkg2=wbf["Wkg2"], Wv=wbf["Wv"], Wg=wbf["Wg"],
                         ln1c=ln1c, bgc=bgc, bkgc=bkgc, **std))
    r1 = run(nc1, maps)

    def to_streams(key, rows, rr):
        full = [np.concatenate([np.asarray(rr[b * NQ + q][key]) for q in range(NQ)], axis=1)
                for b in range(B)]
        return full

    qf = to_streams("q_o", 1024, r1)
    kf = to_streams("k_o", 1024, r1)
    gkf = to_streams("gk_o", 1024, r1)
    vf = to_streams("v_o", 2048, r1)

    nc2 = build_L2(L)
    maskUT = np.triu(np.ones((128, 128), np.float32))
    ones_sc = np.ones((64, 128), np.float32)
    maps = []
    for c in range(NCORE):
        b = c // 4
        hs = [(c % 4) * 4 + j for j in range(4)]
        maps.append(dict(
            q_i=np.ascontiguousarray(np.stack([qf[b][h * 64:(h + 1) * 64] for h in hs])),
            k_i=np.ascontiguousarray(np.stack([kf[b][h * 64:(h + 1) * 64] for h in hs])),
            gk_i=np.ascontiguousarray(np.stack([gkf[b][h * 64:(h + 1) * 64] for h in hs])),
            v_i=np.ascontiguousarray(np.stack([vf[b][h * 128:(h + 1) * 128] for h in hs])),
            maskUT=maskUT, ones_sc=ones_sc, **std))
    r2 = run(nc2, maps)

    o_fm = [np.empty((2048, L), BF) for _ in range(B)]
    for c in range(NCORE):
        b = c // 4
        hs = [(c % 4) * 4 + j for j in range(4)]
        for j, h in enumerate(hs):
            o_fm[b][h * 128:(h + 1) * 128, :] = np.asarray(r2[c]["o_o"])[j].T

    nc3 = build_L3(TLOC)
    maps = []
    for c in range(NCORE):
        b, t0 = core_bt(c)
        maps.append(dict(xT=xTs[c], o_i=np.ascontiguousarray(o_fm[b][:, t0:t0 + TLOC]),
                         sg_i=np.asarray(r1[c]["sg_o"]), Wo=wbf["Wo"], W1=wbf["W1"],
                         W3=wbf["W3"], W2=wbf["W2"], ln2c=ln2c, **std))
    r3 = run(nc3, maps)

    nc4 = build_L4(TLOC)
    maps = []
    for c in range(NCORE):
        maps.append(dict(x_i=np.asarray(r3[c]["x1_o"]), gT=gT, uT=uT, dT=dT, **std))
    r4 = run(nc4, maps)

    nc5 = build_L5(TLOC)
    maps = []
    for c in range(NCORE):
        maps.append(dict(x_i=np.asarray(r4[c]["x2_o"]), WqkT=wbf["qkvuT"], **std))
    r5 = run(nc5, maps)

    q2 = to_streams("q_o", 2048, r5)
    k2 = to_streams("k_o", 2048, r5)
    v2 = to_streams("v_o", 2048, r5)

    nc6 = build_L6(L)
    delta = np.empty((128, 128), np.float32)
    for s in range(128):
        for t in range(128):
            delta[s, t] = (t - s) if t >= s else 1e30
    iota_f = np.arange(128, dtype=np.float32).reshape(1, 128)
    iota_r = (128.0 - np.arange(128, dtype=np.float32)).reshape(1, 128)
    slope_h = np.asarray(L1_attn_slope_rate, np.float32).reshape(16)
    maps = []
    for c in range(NCORE):
        b = c // 4
        hs = [(c % 4) * 4 + j for j in range(4)]
        slope = np.ascontiguousarray(
            np.broadcast_to(slope_h[hs][:, None], (4, 128)).astype(np.float32))
        maps.append(dict(
            q_i=np.ascontiguousarray(np.stack([q2[b][h * 128:(h + 1) * 128] for h in hs])),
            k_i=np.ascontiguousarray(np.stack([k2[b][h * 128:(h + 1) * 128] for h in hs])),
            v_i=np.ascontiguousarray(np.stack([v2[b][h * 128:(h + 1) * 128] for h in hs])),
            delta=delta, iota_f=iota_f, iota_r=iota_r, slope=slope, **std))
    r6 = run(nc6, maps)

    o2_fm = [np.empty((2048, L), BF) for _ in range(B)]
    for c in range(NCORE):
        b = c // 4
        hs = [(c % 4) * 4 + j for j in range(4)]
        for j, h in enumerate(hs):
            o2_fm[b][h * 128:(h + 1) * 128, :] = np.asarray(r6[c]["o_o"])[j].T

    nc7 = build_L7(TLOC)
    maps = []
    for c in range(NCORE):
        b, t0 = core_bt(c)
        maps.append(dict(x_i=np.asarray(r4[c]["x2_o"]),
                         o_i=np.ascontiguousarray(o2_fm[b][:, t0:t0 + TLOC]),
                         u_i=np.asarray(r5[c]["u_o"]),
                         WoT=wbf["outT"], WgT=wbf["gT1"], WuT=wbf["uT1"], WdT=wbf["dT1"],
                         **std))
    r7 = run(nc7, maps)

    out = np.empty((B, L, 2048), np.float32)
    for c in range(NCORE):
        b, t0 = core_bt(c)
        out[b, t0:t0 + TLOC, :] = np.asarray(r7[c]["x4_o"]).T
    return out
